# version 03 (final)
# baseline (speedup 1.0000x reference)
import os
import numpy as np
import ml_dtypes
from contextlib import ExitStack

K_STAGE = os.environ.get("K_STAGE", "")
K_CUT = int(os.environ.get("K_CUT", "99"))
K_STAT = int(os.environ.get("K_STAT", "99"))

import concourse.bass as bass
import concourse.mybir as mybir
from concourse.bass_utils import run_bass_kernel_spmd
from concourse.bass_core_rust import IndirectOffsetOnAxis

F32 = mybir.dt.float32
BF16 = mybir.dt.bfloat16
I32 = mybir.dt.int32
U32 = mybir.dt.uint32
AF = mybir.ActivationFunctionType
ALU = mybir.AluOpType
AX = mybir.AxisListType

NCORE = 8
B, S, D = 2, 8192, 1024
E2 = 2048
NTOK = B * S
GT = NTOK // 128
TPC = NTOK // NCORE
RT = TPC // 128
PT = RT + 1
PADROWS = PT * 128
NE = 65536
HK, KD, NK = 8, 128, 256
NSLOT = 64
GELC = 0.7978845608028654

ENGS = ("sync", "vector", "scalar", "gpsimd", "tensor")


def build_nc():
    nc = bass.Bass("TRN2", target_bir_lowering=False, debug=False)

    xf = nc.dram_tensor("xf", [NTOK, D], F32, kind="ExternalInput").ap()
    xr_in = nc.dram_tensor("xr_in", [TPC, D], F32, kind="ExternalInput").ap()
    smap = nc.dram_tensor("smap", [128, GT], I32, kind="ExternalInput").ap()
    wqT = nc.dram_tensor("wqT", [D, D], F32, kind="ExternalInput").ap()
    wkT = nc.dram_tensor("wkT", [D, D], F32, kind="ExternalInput").ap()
    wvT = nc.dram_tensor("wvT", [D, D], F32, kind="ExternalInput").ap()
    wqf = nc.dram_tensor("wqf", [D, E2], F32, kind="ExternalInput").ap()
    ktr = nc.dram_tensor("ktr", [2, HK, KD, NK], F32, kind="ExternalInput").ap()
    bq_d = nc.dram_tensor("bq_d", [1, D], F32, kind="ExternalInput").ap()
    bk_d = nc.dram_tensor("bk_d", [1, D], F32, kind="ExternalInput").ap()
    bv_d = nc.dram_tensor("bv_d", [1, D], F32, kind="ExternalInput").ap()
    gcol_d = nc.dram_tensor("gcol_d", [128, 16], F32, kind="ExternalInput").ap()
    bcol_d = nc.dram_tensor("bcol_d", [128, 16], F32, kind="ExternalInput").ap()
    embd = nc.dram_tensor("embd", [NE, D], F32, kind="ExternalInput").ap()
    embu = nc.dram_tensor("embu", [NE, D], F32, kind="ExternalInput").ap()
    out_d = nc.dram_tensor("out", [TPC, D], F32, kind="ExternalOutput").ap()
    xnd = nc.dram_tensor("xnd", [NTOK, D], BF16, kind="Internal").ap()
    xnsh = nc.dram_tensor("xnsh", [PADROWS + 1, D], BF16, kind="Internal").ap()

    ops = {k: [] for k in ENGS}
    ctr = {}
    semh = {}

    def emit(eng, fn, waits=(), inc=None, incn=1):
        if inc is not None:
            ctr[inc] = ctr.get(inc, 0) + incn
        w = tuple((s, v) for (s, v) in waits if v and v > 0)

        def run(e, _fn=fn, _w=w, _inc=inc, _incn=incn):
            for s, v in _w:
                e.wait_ge(semh[s], v)
            ins = _fn(e)
            if _inc is not None and ins is not None:
                ins.then_inc(semh[_inc], _incn)
        ops[eng].append(run)

    def cur(s):
        return ctr.get(s, 0)

    SEMS = (["LD", "PLD", "SC", "PE", "DV", "AC", "PO"]
            + [f"SP{i}" for i in range(4)]
            + [f"SC{i}" for i in range(4)]
            + [f"LX{i}" for i in range(4)]
            + [f"LP{i}" for i in range(2)]
            + [f"LR{i}" for i in range(3)]
            + [f"XR{i}" for i in range(2)]
            + [f"GS{i}" for i in range(2)]
            + [f"SA{i}" for i in range(2)]
            + [f"ST{i}" for i in range(2)]
            + [f"GW{i}" for i in range(3)]
            + [f"GU{i}" for i in range(3)])

    with ExitStack() as ctx:
        e = ctx.enter_context
        for s in SEMS:
            semh[s] = e(nc.semaphore(s))

        wq_sb = e(nc.sbuf_tensor("wq_sb", [128, 8 * D], BF16))
        wk_sb = e(nc.sbuf_tensor("wk_sb", [128, 8 * D], BF16))
        wv_sb = e(nc.sbuf_tensor("wv_sb", [128, 8 * D], BF16))
        wqf_sb = e(nc.sbuf_tensor("wqf_sb", [128, 8 * E2], BF16))
        keyS_sb = e(nc.sbuf_tensor("keyS_sb", [128, 16 * NK], BF16))
        bias_sb = e(nc.sbuf_tensor("bias_sb", [1, 16 * NK], BF16))
        bq_sb = e(nc.sbuf_tensor("bq_sb", [1, D], BF16))
        bk_sb = e(nc.sbuf_tensor("bk_sb", [1, D], BF16))
        bv_sb = e(nc.sbuf_tensor("bv_sb", [1, D], BF16))
        gcol_sb = e(nc.sbuf_tensor("gcol_sb", [128, 16], F32))
        bcol_sb = e(nc.sbuf_tensor("bcol_sb", [128, 16], F32))
        Gc_sb = e(nc.sbuf_tensor("Gc_sb", [128, 16], F32))
        Bvc_sb = e(nc.sbuf_tensor("Bvc_sb", [128, 16], F32))
        smap_sb = e(nc.sbuf_tensor("smap_sb", [128, GT], I32))
        ones_bf = e(nc.sbuf_tensor("ones_bf", [1, 128], BF16))
        onescol_bf = e(nc.sbuf_tensor("onescol_bf", [128, 1], BF16))
        onescol_f = e(nc.sbuf_tensor("onescol_f", [128, 1], F32))
        eps_t = e(nc.sbuf_tensor("eps_t", [128, 1], F32))
        one_t = e(nc.sbuf_tensor("one_t", [128, 1], F32))
        dg_bf = e(nc.sbuf_tensor("dg_bf", [128, 128], BF16))
        zt = e(nc.sbuf_tensor("zt", [128, D], BF16))

        ARENA_B = 104832
        arena = e(nc.sbuf_tensor("arena", [128, ARENA_B // 2], BF16))
        _cursor = [0]

        def carve(nelem, dt):
            sz = {F32: 4, BF16: 2, I32: 4, U32: 4}[dt]
            off = _cursor[0]
            _cursor[0] = off + ((nelem * sz + 63) // 64) * 64
            assert _cursor[0] <= ARENA_B, "arena overflow"
            sl = arena[:, off // 2: off // 2 + (nelem * sz) // 2]
            return sl if dt == BF16 else sl.bitcast(dt)

        pbank = [e(nc.psum_tensor(f"pb{i}", [128, 512], F32)) for i in range(8)]
        io1 = carve(128, I32)
        io2 = carve(128, I32)
        _cursor[0] = 0

        emit("gpsimd", lambda g: g.memset(zt[:], 0.0))
        emit("gpsimd", lambda g: g.memset(ones_bf[:], 1.0))
        emit("gpsimd", lambda g: g.memset(onescol_bf[:], 1.0))
        emit("gpsimd", lambda g: g.memset(onescol_f[:], 1.0))
        emit("gpsimd", lambda g: g.memset(eps_t[:], 1e-5))
        emit("gpsimd", lambda g: g.memset(one_t[:], 1.0))
        emit("gpsimd", lambda g: g.iota(out=io1, pattern=[[1, 128]], base=0,
                                        channel_multiplier=0,
                                        allow_small_or_imprecise_dtypes=True))
        emit("gpsimd", lambda g: g.iota(out=io2, pattern=[[0, 128]], base=0,
                                        channel_multiplier=1,
                                        allow_small_or_imprecise_dtypes=True),
             inc="PO")
        PO_IOTA = cur("PO")
        emit("vector", lambda v: v.tensor_tensor(out=dg_bf[:], in0=io1,
                                                 in1=io2, op=ALU.is_equal),
             waits=[("PO", PO_IOTA)], inc="DV")
        DG_DV = cur("DV")

        def wload(dst, src):
            emit("gpsimd", lambda g, d=dst, s=src: g.dma_start(out=d, in_=s),
                 waits=[("PO", PO_IOTA)], inc="PLD", incn=16)
        wload(wq_sb[:].rearrange("p (c o) -> p c o", c=8),
              wqT.rearrange("(c p) o -> p c o", p=128))
        wload(wk_sb[:].rearrange("p (c o) -> p c o", c=8),
              wkT.rearrange("(c p) o -> p c o", p=128))
        wload(wv_sb[:].rearrange("p (c o) -> p c o", c=8),
              wvT.rearrange("(c p) o -> p c o", p=128))
        wload(wqf_sb[:].rearrange("p (c o) -> p c o", c=8),
              wqf.rearrange("(c p) o -> p c o", p=128))
        wload(bq_sb[:], bq_d)
        wload(bk_sb[:], bk_d)
        wload(bv_sb[:], bv_d)
        W_PLD_BOX = [cur("PLD")]
        emit("gpsimd", lambda g: g.dma_start(out=xnsh[0:15, :], in_=zt[0:15, :]),
             inc="SC", incn=16)
        emit("gpsimd", lambda g: g.dma_start(out=xnsh[RT * 128 + 15:PADROWS, :],
                                             in_=zt[0:113, :]),
             inc="SC", incn=16)
        SC_ZF = cur("SC")
        emit("sync", lambda s: s.dma_start(out=gcol_sb[:], in_=gcol_d),
             inc="LD", incn=16)
        emit("sync", lambda s: s.dma_start(out=bcol_sb[:], in_=bcol_d),
             inc="LD", incn=16)
        emit("sync", lambda s: s.dma_start(out=smap_sb[:], in_=smap),
             inc="LD", incn=16)
        LD_PRE = cur("LD")

        PE_T0 = {}
        SP_T = {}
        SC_T = {}
        if True:
            x0 = [carve(D, F32) for i in range(4)]
            xnb = [carve(D, BF16) for i in range(4)]
            junkb = carve(D, BF16)
            nm4 = [carve(4, F32) for i in range(2)]
            ss4 = [carve(4, F32) for i in range(2)]
            va4 = [carve(4, F32) for i in range(2)]
            rs4 = [carve(4, F32) for i in range(2)]
            nr4 = [carve(4, F32) for i in range(2)]
            C_sb = carve(8 * D, F32)
            keyf_sb = carve(16 * NK, F32)
            ps_xs = pbank[6][:, 0:8]
            ps_g = [(pbank[2 * i][:], pbank[2 * i + 1][:]) for i in range(3)]

            wload(keyf_sb[:].rearrange("p (a h k) -> p a h k", a=2, h=HK),
                  ktr.rearrange("a h d k -> d a h k"))
            KEY_PLD = cur("PLD")
            W_PLD_BOX[0] = KEY_PLD

            AC_XN = {}
            LD_X = {}
            for m in range(GT if K_CUT >= 2 else 0):
                emit("sync", lambda s, m=m: s.dma_start(
                    out=x0[m % 4], in_=xf[m * 128:(m + 1) * 128, :]),
                    waits=[("DV", DG_DV if m < 4 else 0),
                           ("AC", AC_XN.get(m - 4, 0))],
                    inc=f"LX{m % 4}", incn=16)
                LD_X[m] = (f"LX{m % 4}", cur(f"LX{m % 4}"))
                qp = (m // 4) % 2
                u = m % 4
                emit("vector", lambda v, m=m, qp=qp, u=u: v.reduce_sum(
                    out=nm4[qp][:, u:u + 1], in_=x0[m % 4], axis=AX.X,
                    negate=True),
                    waits=[LD_X[m],
                           ("DV", cur("DV") if u == 0 else 0)], inc="DV")
                emit("scalar", lambda s, m=m, qp=qp, u=u: s.activation(
                    out=junkb, in_=x0[m % 4], func=AF.Square,
                    accum_out=ss4[qp][:, u:u + 1]),
                    waits=[LD_X[m], ("AC", cur("AC"))], inc="AC")
                if u < 3:
                    continue
                q = m // 4
                AC_SS = cur("AC")
                DV_NM = cur("DV")
                emit("vector", lambda v, qp=qp: v.tensor_scalar(
                    out=nm4[qp], in0=nm4[qp], scalar1=1.0 / D,
                    scalar2=None, op0=ALU.mult),
                    waits=[("DV", DV_NM)], inc="DV")
                emit("vector", lambda v, qp=qp: v.tensor_tensor(
                    out=va4[qp], in0=nm4[qp], in1=nm4[qp],
                    op=ALU.mult), waits=[("DV", cur("DV"))], inc="DV")
                emit("vector", lambda v, qp=qp: v.scalar_tensor_tensor(
                    out=va4[qp], in0=ss4[qp], scalar=1.0 / D,
                    in1=va4[qp], op0=ALU.mult, op1=ALU.subtract),
                    waits=[("AC", AC_SS), ("DV", cur("DV"))], inc="DV")
                DV_VAR = cur("DV")
                emit("scalar", lambda s, qp=qp: s.activation(
                    out=va4[qp], in_=va4[qp], func=AF.Ln, bias=eps_t[:]),
                    waits=[("DV", DV_VAR)], inc="AC")
                emit("scalar", lambda s, qp=qp: s.activation(
                    out=rs4[qp], in_=va4[qp], func=AF.Exp, scale=-0.5),
                    waits=[("AC", cur("AC"))], inc="AC")
                AC_RS = cur("AC")
                emit("vector", lambda v, qp=qp: v.tensor_tensor(
                    out=nr4[qp], in0=nm4[qp], in1=rs4[qp],
                    op=ALU.mult), waits=[("AC", AC_RS), ("DV", cur("DV"))],
                    inc="DV")
                DV_NMR = cur("DV")
                for u2 in range(4):
                    m2 = 4 * q + u2
                    par = m2 % 4
                    emit("scalar", lambda s, par=par, qp=qp, u2=u2: s.activation(
                        out=xnb[par], in_=x0[par], func=AF.Identity,
                        bias=nr4[qp][:, u2:u2 + 1], scale=rs4[qp][:, u2:u2 + 1]),
                        waits=[("DV", DV_NMR),
                               ("PE", PE_T0.get(m2 - 4, 0))]
                        + ([SP_T[m2 - 4]] if (m2 - 4) in SP_T else [])
                        + ([SC_T[m2 - 4]] if (m2 - 4) in SC_T else []),
                        inc="AC")
                    AC_XN[m2] = cur("AC")
                    emit("sync", lambda s, par=par, m2=m2: s.dma_start(
                        out=xnd[m2 * 128:(m2 + 1) * 128, :], in_=xnb[par]),
                        waits=[("AC", AC_XN[m2])], inc=f"SP{m2 % 4}", incn=16)
                    SP_T[m2] = (f"SP{m2 % 4}", cur(f"SP{m2 % 4}"))
                    emit("gpsimd", lambda g, par=par, m2=m2: g.indirect_dma_start(
                        out=xnsh, out_offset=IndirectOffsetOnAxis(
                            smap_sb[:, m2:m2 + 1], 0),
                        in_=xnb[par], in_offset=None),
                        waits=[("AC", AC_XN[m2]), ("SC", SC_ZF),
                               ("LD", LD_PRE), ("PLD", KEY_PLD)]
                        + ([SC_T[m2 - 1]] if (m2 - 1) in SC_T else []),
                        inc=f"SC{m2 % 4}", incn=16)
                    SC_T[m2] = (f"SC{m2 % 4}", cur(f"SC{m2 % 4}"))
                    for c in range(3):
                        st = (m2 == 0)
                        fin = (m2 == GT - 1)
                        emit("tensor", lambda t, par=par, c=c, m2=m2: t.ldweights(
                            xnb[par][:, c * 128:(c + 1) * 128]),
                            waits=[("AC", AC_XN[m2])] if c == 0 else ())
                        emit("tensor", lambda t, par=par, c=c, st=st, fin=fin: t.matmul(
                            out=ps_g[c][0],
                            lhsT=xnb[par][:, c * 128:(c + 1) * 128],
                            rhs=xnb[par][:, 0:512], start=st, stop=fin))
                        emit("tensor", lambda t, par=par, c=c, st=st, fin=fin: t.matmul(
                            out=ps_g[c][1],
                            lhsT=xnb[par][:, c * 128:(c + 1) * 128],
                            rhs=xnb[par][:, 512:1024], start=st, stop=fin))
                        emit("tensor", lambda t, par=par, c=c, st=st: t.matmul(
                            out=ps_xs[:, c:c + 1],
                            lhsT=xnb[par][:, c * 128:(c + 1) * 128],
                            rhs=onescol_bf[:], start=(st and c == 0),
                            stop=False), inc="PE")
                    PE_T0[m2] = cur("PE")
            PE_P0A = cur("PE")
            for c in range(3 if K_CUT >= 2 else 0):
                for hh in range(2):
                    emit("vector", lambda v, c=c, hh=hh: v.tensor_copy(
                        out=C_sb[:, c * D + hh * 512:c * D + (hh + 1) * 512],
                        in_=ps_g[c][hh]),
                        waits=[("PE", PE_P0A), ("DV", cur("DV"))], inc="DV")
            DV_EVAC = cur("DV")

            SC_ALL = [(f"SC{i}", cur(f"SC{i}")) for i in range(4)]
            for pidx, chunks in [(1, (3, 4, 5)), (2, (6, 7))][:max(0, K_CUT - 2)]:
                PREV_PE_ALL = cur("PE")
                PE_T0 = {}
                LD_R = {}
                dv_gate = DV_EVAC
                for m in range(GT):
                    par = m % 4
                    emit("sync", lambda s, par=par, m=m: s.dma_start(
                        out=xnb[par], in_=xnd[m * 128:(m + 1) * 128, :]),
                        waits=[SP_T[m],
                               ("PE", PREV_PE_ALL if m < 4 else PE_T0.get(m - 4, 0))]
                        + ([SP_T[GT - 4 + m]] if m < 4 else [])
                        + (SC_ALL if (pidx == 1 and m < 4) else []),
                        inc=f"LX{m % 4}", incn=16)
                    LD_R[m] = (f"LX{m % 4}", cur(f"LX{m % 4}"))
                    for ci, c in enumerate(chunks):
                        gi = ci
                        st = (m == 0)
                        lastxs = (pidx == 2 and m == GT - 1 and ci == 1)
                        emit("tensor", lambda t, par=par, c=c, m=m: t.ldweights(
                            xnb[par][:, c * 128:(c + 1) * 128]),
                            waits=[LD_R[m]] + (
                                [("DV", dv_gate)] if m == 0 else []))
                        fin = (m == GT - 1)
                        emit("tensor", lambda t, par=par, c=c, gi=gi, st=st, fin=fin: t.matmul(
                            out=ps_g[gi][0],
                            lhsT=xnb[par][:, c * 128:(c + 1) * 128],
                            rhs=xnb[par][:, 0:512], start=st, stop=fin))
                        emit("tensor", lambda t, par=par, c=c, gi=gi, st=st, fin=fin: t.matmul(
                            out=ps_g[gi][1],
                            lhsT=xnb[par][:, c * 128:(c + 1) * 128],
                            rhs=xnb[par][:, 512:1024], start=st, stop=fin))
                        emit("tensor", lambda t, par=par, c=c, lastxs=lastxs: t.matmul(
                            out=ps_xs[:, c:c + 1],
                            lhsT=xnb[par][:, c * 128:(c + 1) * 128],
                            rhs=onescol_bf[:], start=False, stop=lastxs),
                            inc="PE")
                    PE_T0[m] = cur("PE")
                PE_PX = cur("PE")
                for ci, c in enumerate(chunks):
                    for hh in range(2):
                        emit("vector", lambda v, c=c, ci=ci, hh=hh: v.tensor_copy(
                            out=C_sb[:, c * D + hh * 512:c * D + (hh + 1) * 512],
                            in_=ps_g[ci][hh]),
                            waits=[("PE", PE_PX), ("DV", cur("DV"))], inc="DV")
                DV_EVAC = cur("DV")
            PE_XS = cur("PE")

            _cursor[0] = 0
            class _StatCut(Exception):
                pass
            try:
                Cbf = carve(8 * D, BF16)
                xsb = carve(8, BF16)
                prod = carve(512, F32)
                meanc = carve(16, F32)
                q2c = carve(16, F32)
                varc = carve(16, F32)
                rstc = carve(16, F32)
                ps_mean = pbank[0][:, 0:16]
                ps_q2 = pbank[1][:, 0:16]
                ps_T = [pbank[2][:], pbank[3][:]]
                ps_bias = pbank[4][0:1, 0:256]

                if K_CUT < 5:
                    STATS_AC = cur("AC"); STATS_DV = cur("DV"); STATS_PE = cur("PE")
                    raise _StatCut
                if os.environ.get("K_SUB", "") != "noc":
                    for c in range(8):
                        emit("vector", lambda v, c=c: v.tensor_copy(
                            out=Cbf[:, c * D:(c + 1) * D],
                            in_=C_sb[:, c * D:(c + 1) * D]),
                            waits=[("DV", DV_EVAC)], inc="DV")
                if os.environ.get("K_SUB", "") != "nox":
                    emit("vector", lambda v: v.tensor_copy(out=xsb, in_=ps_xs),
                         waits=[("PE", PE_XS), ("DV", cur("DV"))], inc="DV")
                DV_CBF = cur("DV")

                first = True
                for ec in range(16 if K_STAT >= 2 else 0):
                    for dc in range(8):
                        emit("tensor", lambda t, ec=ec, dc=dc: t.ldweights(
                            wqf_sb[:, dc * E2 + ec * 128:dc * E2 + (ec + 1) * 128]),
                            waits=[("DV", DV_CBF)] if first else ())
                        emit("tensor", lambda t, ec=ec, dc=dc, f=first: t.matmul(
                            out=ps_mean[0:128, ec:ec + 1],
                            lhsT=wqf_sb[:, dc * E2 + ec * 128:dc * E2 + (ec + 1) * 128],
                            rhs=xsb[0:128, dc:dc + 1], start=f,
                            stop=(ec == 15 and dc == 7)),
                            inc=("PE" if ec == 15 and dc == 7 else None))
                        first = False
                PE_MEAN = cur("PE")

                first_q2 = True
                PE_Q2MM = 0
                for dc in range(8 if K_STAT >= 3 else 0):
                    for eq in range(4):
                        tp = (dc * 4 + eq) % 2
                        for dp in range(8):
                            emit("tensor", lambda t, dc=dc, dp=dp: t.ldweights(
                                Cbf[:, dp * D + dc * 128:dp * D + (dc + 1) * 128]),
                                waits=[("DV", DV_CBF)] if dc == 0 and eq == 0
                                and dp == 0 else ())
                            emit("tensor", lambda t, dc=dc, dp=dp, eq=eq, tp=tp:
                                 t.matmul(
                                     out=ps_T[tp],
                                     lhsT=Cbf[:, dp * D + dc * 128:
                                              dp * D + (dc + 1) * 128],
                                     rhs=wqf_sb[:, dp * E2 + eq * 512:
                                                dp * E2 + (eq + 1) * 512],
                                     start=(dp == 0), stop=(dp == 7)),
                                 inc=("PE" if dp == 7 else None))
                        PE_T_ = cur("PE")
                        emit("vector", lambda v, dc=dc, eq=eq, tp=tp: v.tensor_tensor(
                            out=prod,
                            in0=wqf_sb[:, dc * E2 + eq * 512:dc * E2 + (eq + 1) * 512],
                            in1=ps_T[tp], op=ALU.mult),
                            waits=[("PE", PE_T_), ("PE", PE_Q2MM),
                                   ("DV", cur("DV"))], inc="DV")
                        DV_PROD = cur("DV")
                        for eb in range(4):
                            ecol = eq * 4 + eb
                            emit("tensor", lambda t, eb=eb, ecol=ecol, f=first_q2,
                                 dc=dc, eq=eq:
                                 t.matmul(
                                     out=ps_q2[0:128, ecol:ecol + 1],
                                     lhsT=prod[:, eb * 128:(eb + 1) * 128],
                                     rhs=onescol_f[:], start=f,
                                     stop=(dc == 7 and eq == 3 and eb == 3)),
                                 waits=[("DV", DV_PROD)], inc="PE")
                            first_q2 = False
                        PE_Q2MM = cur("PE")
                PE_Q2 = cur("PE")

                if K_STAT < 4:
                    STATS_AC = cur("AC"); STATS_DV = cur("DV"); STATS_PE = cur("PE")
                    raise _StatCut
                emit("vector", lambda v: v.tensor_scalar(
                    out=meanc, in0=ps_mean, scalar1=1.0 / NTOK,
                    scalar2=None, op0=ALU.mult),
                    waits=[("PE", PE_MEAN), ("DV", cur("DV"))], inc="DV")
                emit("vector", lambda v: v.tensor_scalar(
                    out=q2c, in0=ps_q2, scalar1=1.0 / NTOK, scalar2=None,
                    op0=ALU.mult),
                    waits=[("PE", PE_Q2), ("DV", cur("DV"))], inc="DV")
                emit("vector", lambda v: v.tensor_tensor(
                    out=varc, in0=meanc, in1=meanc, op=ALU.mult),
                    waits=[("DV", cur("DV"))], inc="DV")
                emit("vector", lambda v: v.tensor_tensor(
                    out=varc, in0=q2c, in1=varc, op=ALU.subtract),
                    waits=[("DV", cur("DV"))], inc="DV")
                DV_VARC = cur("DV")
                emit("scalar", lambda s: s.activation(
                    out=varc, in_=varc, func=AF.Ln, bias=eps_t[:]),
                    waits=[("DV", DV_VARC)], inc="AC")
                emit("scalar", lambda s: s.activation(
                    out=rstc, in_=varc, func=AF.Exp, scale=-0.5),
                    waits=[("AC", cur("AC"))], inc="AC")
                AC_RSTC = cur("AC")
                emit("vector", lambda v: v.tensor_tensor(
                    out=Gc_sb[:], in0=gcol_sb[:], in1=rstc, op=ALU.mult),
                    waits=[("AC", AC_RSTC), ("LD", LD_PRE), ("DV", cur("DV"))],
                    inc="DV")
                emit("vector", lambda v: v.tensor_tensor(
                    out=Bvc_sb[:], in0=meanc, in1=Gc_sb[:], op=ALU.mult),
                    waits=[("DV", cur("DV"))], inc="DV")
                emit("vector", lambda v: v.scalar_tensor_tensor(
                    out=Bvc_sb[:], in0=Bvc_sb[:], scalar=-1.0, in1=bcol_sb[:],
                    op0=ALU.mult, op1=ALU.add),
                    waits=[("DV", cur("DV"))], inc="DV")
                DV_BVC = cur("DV")
                if K_STAT < 5:
                    STATS_AC = cur("AC"); STATS_DV = cur("DV"); STATS_PE = cur("PE")
                    raise _StatCut
                for ph in range(16):
                    emit("vector", lambda v, ph=ph: v.tensor_scalar(
                        out=keyS_sb[:, ph * NK:(ph + 1) * NK],
                        in0=keyf_sb[:, ph * NK:(ph + 1) * NK],
                        scalar1=Gc_sb[:, ph:ph + 1], scalar2=None, op0=ALU.mult),
                        waits=[("DV", DV_BVC), ("PLD", KEY_PLD)], inc="DV")
                DV_KEYS = cur("DV")
                if K_STAT < 6:
                    STATS_AC = cur("AC"); STATS_DV = cur("DV"); STATS_PE = cur("PE")
                    raise _StatCut
                for ph in range(16):
                    emit("tensor", lambda t, ph=ph: t.matmul(
                        out=ps_bias, lhsT=Bvc_sb[:, ph:ph + 1],
                        rhs=keyf_sb[:, ph * NK:(ph + 1) * NK],
                        start=True, stop=True),
                        waits=([("DV", DV_KEYS)] if ph == 0 else
                               [("AC", cur("AC"))]), inc="PE")
                    pe_b = cur("PE")
                    emit("scalar", lambda s, ph=ph: s.activation(
                        out=bias_sb[:, ph * NK:(ph + 1) * NK], in_=ps_bias,
                        func=AF.Copy), waits=[("PE", pe_b)], inc="AC")
                STATS_AC = cur("AC")
                STATS_DV = cur("DV")
                STATS_PE = cur("PE")
            except _StatCut:
                STATS_AC = cur("AC")
                STATS_DV = cur("DV")
                STATS_PE = cur("PE")

        _cursor[0] = 0
        xnp_sb = [carve(D, BF16) for i in range(2)]
        xnr_sb = [carve(D, BF16) for i in range(3)]
        xr_sb = [carve(D, F32) for i in range(2)]
        _xTp = carve(8 * 128, BF16)
        xTp_sb = [_xTp, _xTp]
        _xTr = carve(8 * 128, BF16)
        xTr_sb = [_xTr, _xTr]
        q_sb = [carve(D, BF16) for i in range(2)]
        k_sb = [carve(D, BF16) for i in range(2)]
        v_sb = [carve(D, BF16) for i in range(2)]
        G_sb = [carve(D, BF16) for i in range(2)]
        M_sb = carve(2048, BF16)
        Sx_sb = carve(256, F32)
        S2_sb = carve(256, F32)
        mx_sb = carve(16, F32)
        dn_sb = carve(16, F32)
        A_sb = carve(256, BF16)
        O_sb = [carve(D, F32) for i in range(3)]
        qfT_sb = [carve(16 * 128, BF16) for i in range(2)]
        simb = [carve(NK, F32) for i in range(4)]
        sx_sb = carve(64, F32)
        sy_sb = carve(64, F32)
        ixu_sb = carve(64, U32)
        iyu_sb = carve(64, U32)
        ixf_sb = carve(64, F32)
        iyf_sb = carve(64, F32)
        cand_sb = carve(512, F32)
        s8_sb = carve(64, F32)
        pku_sb = carve(64, U32)
        jf_sb = carve(64, F32)
        if_sb = carve(64, F32)
        eqb_sb = carve(64 * 8, F32)
        term_sb = carve(64 * 8, F32)
        ef_sb = [carve(64, F32) for i in range(2)]
        ei_sb = [carve(64, I32) for i in range(2)]
        sg_sb = [carve(64, F32) for i in range(2)]
        se_sb = carve(64, F32)
        zd_sb = carve(8, F32)
        h_sb = carve(64, F32)
        g1_sb = carve(64, F32)
        g2_sb = carve(64, F32)
        g3_sb = carve(64, F32)
        ht_sb = carve(64, F32)
        gbw = [carve(D, BF16) for i in range(3)]
        gbu = [carve(D, BF16) for i in range(3)]
        D_sb = [carve(128, BF16) for i in range(2)]
        outp_sb = [carve(D, F32) for i in range(2)]

        ps_tp = [pbank[0][:].bitcast(BF16), pbank[1][:].bitcast(BF16)]
        ps_c = pbank[2][:]
        ps_d = pbank[3][:]
        ps_f = [(pbank[4][:], pbank[5][:]), (pbank[6][:], pbank[7][:])]

        LD_XNP = {}; LD_XNR = {}; LD_XR = {}
        AC_Q = {}; AC_K = {}; AC_V = {}
        AC_QFT = {}
        GS_CNT = {}; SA_CNT = {}; ST_CNT = {}
        DV_O = {}; DV_IDX = {}; DV_SIG = {}; DV_HT = {}; DV_FINB = {}
        DV_AFR = {}; DV_STT = {}
        PE_QKV = {}; PE_QFT = {}; PE_SIM = {}; PE_FMM = {}; PE_FMM_T = {}
        STT_J = {}; GW_J = {}; GU_J = {}
        bank_ac = {"tp0": 0, "tp1": 0, "c": 0, "d": 0}
        bank_dv = {"c": 0, "d": 0}

        def load_stage(i):
            if 0 <= i <= PT - 1:
                emit("sync", lambda s, i=i: s.dma_start(
                    out=xnp_sb[i % 2][:], in_=xnsh[i * 128:(i + 1) * 128, :]),
                    waits=[("PE", PE_QKV.get(i - 2, 0)), ("AC", STATS_AC)]
                    + [SC_T[k] for k in (GT - 4, GT - 3, GT - 2, GT - 1)
                       if k in SC_T], inc=f"LP{i % 2}", incn=16)
                LD_XNP[i] = (f"LP{i % 2}", cur(f"LP{i % 2}"))
            if 0 <= i <= RT - 1:
                emit("sync", lambda s, i=i: s.dma_start(
                    out=xnr_sb[i % 3][:],
                    in_=xnsh[15 + i * 128:15 + (i + 1) * 128, :]),
                    waits=[("DV", DV_STT.get(i - 3, 0))]
                    + [SC_T[k] for k in (GT - 4, GT - 3, GT - 2, GT - 1)
                       if k in SC_T],
                    inc=f"LR{i % 3}", incn=16)
                LD_XNR[i] = (f"LR{i % 3}", cur(f"LR{i % 3}"))

        def xr_load(i):
            if 0 <= i <= RT - 1:
                emit("sync", lambda s, i=i: s.dma_start(
                    out=xr_sb[i % 2][:], in_=xr_in[i * 128:(i + 1) * 128, :]),
                    waits=[("DV", DV_FINB.get(i - 2, 0))],
                    inc=f"XR{i % 2}", incn=16)
                LD_XR[i] = (f"XR{i % 2}", cur(f"XR{i % 2}"))

        def transpose8(src, dst, ld_wait, extra_pe_wait=0):
            for c in range(8):
                bk = c % 2
                emit("tensor", lambda t, c=c, bk=bk, src=src: t.transpose(
                    out=ps_tp[bk][:, 0:128],
                    in_=src[:, c * 128:(c + 1) * 128], identity=dg_bf[:]),
                    waits=[ld_wait, ("DV", DG_DV),
                           ("AC", bank_ac[f"tp{bk}"]),
                           ("PE", extra_pe_wait if c == 0 else 0),
                           ("AC", STATS_AC if c == 0 else 0)],
                    inc="PE")
                pe_t = cur("PE")
                emit("scalar", lambda s, c=c, bk=bk, dst=dst: s.activation(
                    out=dst[:, c * 128:(c + 1) * 128], in_=ps_tp[bk][:, 0:128],
                    func=AF.Copy), waits=[("PE", pe_t)], inc="AC")
                bank_ac[f"tp{bk}"] = cur("AC")

        def qkv_stage(i):
            if not (0 <= i <= PT - 1):
                return
            par = i % 2
            transpose8(xnp_sb[par][:], xTp_sb[par][:], LD_XNP[i],
                       extra_pe_wait=PE_SIM.get(i - 2, 0) if False else 0)
            AC_TP = cur("AC")
            for (wsb, bsb, dst, store) in (
                    (wq_sb, bq_sb, q_sb, AC_Q),
                    (wk_sb, bk_sb, k_sb, AC_K),
                    (wv_sb, bv_sb, v_sb, AC_V)):
                for c in range(8):
                    w = [("AC", AC_TP)] if c == 0 else []
                    if c == 0:
                        w += [("AC", bank_ac["c"]), ("AC", bank_ac["d"]),
                              ("DV", bank_dv["c"]), ("DV", bank_dv["d"]),
                              ("PLD", W_PLD_BOX[0])]
                    emit("tensor", lambda t, par=par, c=c: t.ldweights(
                        xTp_sb[par][:, c * 128:(c + 1) * 128]), waits=w)
                    emit("tensor", lambda t, par=par, c=c, wsb=wsb: t.matmul(
                        out=ps_c[:], lhsT=xTp_sb[par][:, c * 128:(c + 1) * 128],
                        rhs=wsb[:, c * D:c * D + 512], start=(c == 0),
                        stop=False))
                    emit("tensor", lambda t, par=par, c=c, wsb=wsb: t.matmul(
                        out=ps_d[:], lhsT=xTp_sb[par][:, c * 128:(c + 1) * 128],
                        rhs=wsb[:, c * D + 512:c * D + 1024], start=(c == 0),
                        stop=False))
                emit("tensor", lambda t: t.ldweights(ones_bf[:]))
                emit("tensor", lambda t, bsb=bsb: t.matmul(
                    out=ps_c[:], lhsT=ones_bf[:], rhs=bsb[:, 0:512],
                    start=False, stop=True))
                emit("tensor", lambda t, bsb=bsb: t.matmul(
                    out=ps_d[:], lhsT=ones_bf[:], rhs=bsb[:, 512:1024],
                    start=False, stop=True), inc="PE")
                pe_p = cur("PE")
                wv = ([GS_CNT[i - 2]] if (dst is q_sb and (i - 2) in GS_CNT)
                      else [("DV", DV_AFR.get(i - 2, 0))])
                emit("scalar", lambda s, par=par, dst=dst: s.activation(
                    out=dst[par][:, 0:512], in_=ps_c[:], func=AF.Copy),
                    waits=[("PE", pe_p)] + wv,
                    inc="AC")
                bank_ac["c"] = cur("AC")
                emit("scalar", lambda s, par=par, dst=dst: s.activation(
                    out=dst[par][:, 512:1024], in_=ps_d[:], func=AF.Copy),
                    waits=[("PE", pe_p)], inc="AC")
                bank_ac["d"] = cur("AC")
                store[i] = cur("AC")
            PE_QKV[i] = cur("PE")

        def att_strips(i):
            if not (0 <= i <= PT - 1):
                return
            par = i % 2
            gsem = f"GS{i % 2}"
            for h in range(16):
                s0 = 15 - h
                w = [("AC", AC_Q.get(i, 0)), ("AC", AC_Q.get(i + 1, 0)),
                     ("DV", DV_AFR.get(i - 1, 0))] if h == 0 else ()
                emit("sync", lambda s, par=par, h=h, s0=s0: s.dma_start(
                    out=G_sb[par][0:128 - s0, h * 64:(h + 1) * 64],
                    in_=q_sb[par][s0:128, h * 64:(h + 1) * 64]),
                    waits=w, inc=gsem, incn=16)
                if s0 > 0:
                    src = q_sb[(i + 1) % 2] if i + 1 <= PT - 1 else zt
                    emit("sync", lambda s, h=h, s0=s0, src=src: s.dma_start(
                        out=G_sb[par][128 - s0:128, h * 64:(h + 1) * 64],
                        in_=src[0:s0, h * 64:(h + 1) * 64]),
                        inc=gsem, incn=16)
            GS_CNT[i] = (gsem, cur(gsem))

        def att_dots(i):
            if not (0 <= i <= PT - 1):
                return
            par = i % 2
            for qq in range(8):
                g3 = G_sb[par][:, qq * 128:(qq + 1) * 128].rearrange(
                    "p (h f) -> p h f", h=2).unsqueeze(2).to_broadcast(
                    [128, 2, 16, 64])
                k3 = k_sb[par][:].rearrange("p (w f) -> p w f", w=16).unsqueeze(
                    1).to_broadcast([128, 2, 16, 64])
                m4 = M_sb[:].rearrange("p (h w f) -> p h w f", h=2, w=16)
                emit("vector", lambda v, g3=g3, k3=k3, m4=m4: v.tensor_tensor(
                    out=m4, in0=g3, in1=k3, op=ALU.mult),
                    waits=[GS_CNT[i], ("AC", AC_K.get(i, 0)),
                           ("DV", cur("DV"))], inc="DV")
                mr = M_sb[:].rearrange("p (x f) -> p x f", f=64)
                emit("vector", lambda v, mr=mr, qq=qq: v.reduce_sum(
                    out=Sx_sb[:, qq * 32:(qq + 1) * 32], in_=mr, axis=AX.X),
                    waits=[("DV", cur("DV"))], inc="DV")
            emit("vector", lambda v: v.reduce_max(
                out=mx_sb[:], in_=Sx_sb[:].rearrange("p (h w) -> p h w", h=16),
                axis=AX.X), waits=[("DV", cur("DV"))], inc="DV")
            emit("vector", lambda v: v.tensor_tensor(
                out=S2_sb[:].rearrange("p (h w) -> p h w", h=16),
                in0=Sx_sb[:].rearrange("p (h w) -> p h w", h=16),
                in1=mx_sb[:].unsqueeze(2).to_broadcast([128, 16, 16]),
                op=ALU.subtract), waits=[("DV", cur("DV"))], inc="DV")
            DV_SUB = cur("DV")
            emit("scalar", lambda s: s.activation(
                out=S2_sb[:], in_=S2_sb[:], func=AF.Exp, scale=0.125),
                waits=[("DV", DV_SUB)], inc="AC")
            AC_EXP = cur("AC")
            emit("vector", lambda v: v.reduce_sum(
                out=dn_sb[:], in_=S2_sb[:].rearrange("p (h w) -> p h w", h=16),
                axis=AX.X), waits=[("AC", AC_EXP), ("DV", cur("DV"))], inc="DV")
            emit("vector", lambda v: v.reciprocal(out=dn_sb[:], in_=dn_sb[:]),
                 waits=[("DV", cur("DV"))], inc="DV")
            emit("vector", lambda v: v.tensor_tensor(
                out=A_sb[:].rearrange("p (h w) -> p h w", h=16),
                in0=S2_sb[:].rearrange("p (h w) -> p h w", h=16),
                in1=dn_sb[:].unsqueeze(2).to_broadcast([128, 16, 16]),
                op=ALU.mult), waits=[("DV", cur("DV"))], inc="DV")
            DV_A = cur("DV")
            for qq in range(8):
                a3 = A_sb[:, qq * 32:(qq + 1) * 32].rearrange(
                    "p (h w) -> p h w", h=2).unsqueeze(2).to_broadcast(
                    [128, 2, 64, 16])
                v3 = v_sb[par][:].rearrange("p (w f) -> p w f", w=16).transpose(
                    [0, 2, 1]).unsqueeze(1).to_broadcast([128, 2, 64, 16])
                m4 = M_sb[:].rearrange("p (h f w) -> p h f w", h=2, f=64)
                emit("vector", lambda v, a3=a3, v3=v3, m4=m4: v.tensor_tensor(
                    out=m4, in0=a3, in1=v3, op=ALU.mult),
                    waits=[("DV", DV_A if qq == 0 else cur("DV")),
                           ("AC", AC_V.get(i, 0))]
                    + ([SA_CNT[i - 3]] if (qq == 0 and (i - 3) in SA_CNT)
                       else []),
                    inc="DV")
                mr = M_sb[:].rearrange("p (x w) -> p x w", w=16)
                emit("vector", lambda v, mr=mr, qq=qq, i=i: v.reduce_sum(
                    out=O_sb[i % 3][:, qq * 128:(qq + 1) * 128], in_=mr,
                    axis=AX.X), waits=[("DV", cur("DV"))], inc="DV")
            DV_O[i] = cur("DV")
            DV_AFR[i] = cur("DV")

        def ffnh_stage(r):
            if not (0 <= r <= RT - 1):
                return
            par = r % 2
            transpose8(xnr_sb[r % 3][:], xTr_sb[par][:], LD_XNR[r],
                       extra_pe_wait=PE_SIM.get(r - 2, 0))
            AC_TR = cur("AC")
            for ec in range(16):
                bk = "c" if ec % 2 == 0 else "d"
                pst = ps_c if ec % 2 == 0 else ps_d
                for dc in range(8):
                    w = []
                    if dc == 0:
                        w = [("AC", AC_TR), ("AC", bank_ac[bk]),
                             ("DV", bank_dv[bk])]
                    emit("tensor", lambda t, ec=ec, dc=dc: t.ldweights(
                        wqf_sb[:, dc * E2 + ec * 128:dc * E2 + (ec + 1) * 128]),
                        waits=w)
                    emit("tensor", lambda t, par=par, ec=ec, dc=dc, pst=pst:
                         t.matmul(
                             out=pst[:, 0:128],
                             lhsT=wqf_sb[:, dc * E2 + ec * 128:
                                         dc * E2 + (ec + 1) * 128],
                             rhs=xTr_sb[par][:, dc * 128:(dc + 1) * 128],
                             start=(dc == 0), stop=(dc == 7)),
                         inc=("PE" if dc == 7 else None))
                pe_q = cur("PE")
                emit("scalar", lambda s, par=par, ec=ec, pst=pst: s.activation(
                    out=qfT_sb[par][:, ec * 128:(ec + 1) * 128],
                    in_=pst[:, 0:128], func=AF.Copy),
                    waits=[("PE", pe_q), ("PE", PE_SIM.get(r - 2, 0))],
                    inc="AC")
                bank_ac[bk] = cur("AC")
                AC_QFT[(r, ec)] = cur("AC")
            DV_MI_PREV = 0
            for ph in range(16):
                u = ph % 4
                bk = "c" if ph % 2 == 0 else "d"
                pst = ps_c if ph % 2 == 0 else ps_d
                emit("tensor", lambda t, par=par, ph=ph: t.ldweights(
                    qfT_sb[par][:, ph * 128:(ph + 1) * 128]),
                    waits=[("AC", AC_QFT[(r, ph)]), ("AC", bank_ac[bk]),
                           ("DV", bank_dv[bk])])
                emit("tensor", lambda t, par=par, ph=ph, pst=pst: t.matmul(
                    out=pst[:, 0:256],
                    lhsT=qfT_sb[par][:, ph * 128:(ph + 1) * 128],
                    rhs=keyS_sb[:, ph * NK:(ph + 1) * NK], start=True,
                    stop=False))
                emit("tensor", lambda t: t.ldweights(ones_bf[:]))
                emit("tensor", lambda t, ph=ph, pst=pst: t.matmul(
                    out=pst[:, 0:256], lhsT=ones_bf[:],
                    rhs=bias_sb[:, ph * NK:(ph + 1) * NK], start=False,
                    stop=True), inc="PE")
                pe_s = cur("PE")
                emit("scalar", lambda s, u=u, pst=pst: s.activation(
                    out=simb[u][:], in_=pst[:, 0:256], func=AF.Copy),
                    waits=[("PE", pe_s), ("DV", DV_MI_PREV)], inc="AC")
                bank_ac[bk] = cur("AC")
                if u == 3:
                    AC_SIM = cur("AC")
                    base = ph - 3
                    for w2 in range(4):
                        p2, h2 = (base + w2) // 8, (base + w2) % 8
                        dstv = sx_sb if p2 == 0 else sy_sb
                        emit("vector", lambda v, w2=w2, dstv=dstv, h2=h2: v.max(
                            out=dstv[:, h2 * 8:(h2 + 1) * 8], in_=simb[w2][:]),
                            waits=[("AC", AC_SIM), ("DV", cur("DV"))], inc="DV")
                    for w2 in range(4):
                        p2, h2 = (base + w2) // 8, (base + w2) % 8
                        srcv = sx_sb if p2 == 0 else sy_sb
                        dsti = ixu_sb if p2 == 0 else iyu_sb
                        emit("vector",
                             lambda v, w2=w2, srcv=srcv, dsti=dsti, h2=h2:
                             v.max_index(out=dsti[:, h2 * 8:(h2 + 1) * 8],
                                         in_max=srcv[:, h2 * 8:(h2 + 1) * 8],
                                         in_values=simb[w2][:]),
                             waits=[("DV", cur("DV"))] if w2 == 0 else (),
                             inc="DV")
                    DV_MI_PREV = cur("DV")
            PE_SIM[r] = cur("PE")
            DV_TOP1 = cur("DV")
            emit("vector", lambda v: v.tensor_copy(out=ixf_sb[:], in_=ixu_sb[:]),
                 waits=[("DV", DV_TOP1)], inc="DV")
            emit("vector", lambda v: v.tensor_copy(out=iyf_sb[:], in_=iyu_sb[:]),
                 inc="DV")
            emit("vector", lambda v: v.tensor_tensor(
                out=cand_sb[:].rearrange("p (h i j) -> p h i j", h=8, i=8),
                in0=sx_sb[:].rearrange("p (h i) -> p h i", h=8).unsqueeze(3)
                    .to_broadcast([128, 8, 8, 8]),
                in1=sy_sb[:].rearrange("p (h j) -> p h j", h=8).unsqueeze(2)
                    .to_broadcast([128, 8, 8, 8]),
                op=ALU.add), waits=[("DV", cur("DV"))], inc="DV")
            DV_CAND = cur("DV")
            for h2 in range(8):
                emit("vector", lambda v, h2=h2: v.max(
                    out=s8_sb[:, h2 * 8:(h2 + 1) * 8],
                    in_=cand_sb[:, h2 * 64:(h2 + 1) * 64]),
                    waits=[("DV", DV_CAND)] if h2 == 0 else (), inc="DV")
            for h2 in range(8):
                emit("vector", lambda v, h2=h2: v.max_index(
                    out=pku_sb[:, h2 * 8:(h2 + 1) * 8],
                    in_max=s8_sb[:, h2 * 8:(h2 + 1) * 8],
                    in_values=cand_sb[:, h2 * 64:(h2 + 1) * 64]),
                    waits=[("DV", cur("DV"))] if h2 == 0 else (), inc="DV")
            DV_TOP2 = cur("DV")
            emit("vector", lambda v: v.tensor_scalar(
                out=ixu_sb[:], in0=pku_sb[:], scalar1=7, scalar2=None,
                op0=ALU.bitwise_and), waits=[("DV", DV_TOP2)], inc="DV")
            emit("vector", lambda v: v.tensor_scalar(
                out=iyu_sb[:], in0=pku_sb[:], scalar1=3, scalar2=None,
                op0=ALU.logical_shift_right), inc="DV")
            DV_SHIFTS = cur("DV")
            emit("vector", lambda v: v.tensor_copy(out=jf_sb[:], in_=ixu_sb[:]),
                 waits=[("DV", DV_SHIFTS)], inc="DV")
            emit("vector", lambda v: v.tensor_copy(out=if_sb[:], in_=iyu_sb[:]),
                 inc="DV")
            DV_IJ = cur("DV")
            for m in range(8):
                emit("vector", lambda v, m=m: v.tensor_scalar(
                    out=eqb_sb[:, m * 64:(m + 1) * 64], in0=if_sb[:],
                    scalar1=float(m), scalar2=None, op0=ALU.is_equal),
                    waits=[("DV", DV_IJ)] if m == 0 else (), inc="DV")
            for m in range(8):
                ixb = ixf_sb[:, m::8].unsqueeze(2).to_broadcast([128, 8, 8])
                emit("vector", lambda v, m=m, ixb=ixb: v.scalar_tensor_tensor(
                    out=term_sb[:, m * 64:(m + 1) * 64].rearrange(
                        "p (h j) -> p h j", h=8),
                    in0=eqb_sb[:, m * 64:(m + 1) * 64].rearrange(
                        "p (h j) -> p h j", h=8),
                    scalar=256.0, in1=ixb, op0=ALU.mult, op1=ALU.mult),
                    waits=[("DV", cur("DV"))] if m == 0 else (),
                    inc="DV")
            DV_T1 = cur("DV")
            emit("vector", lambda v, r=r: v.reduce_sum(
                out=ef_sb[r % 2][:],
                in_=term_sb[:].rearrange("p (m x) -> p x m", m=8), axis=AX.X),
                waits=[("DV", DV_T1)]
                + ([GU_J[(r - 2, NSLOT - 1)]] if (r - 2, NSLOT - 1) in GU_J
                   else []), inc="DV")
            DV_R1 = cur("DV")
            for m in range(8):
                emit("vector", lambda v, m=m: v.tensor_scalar(
                    out=eqb_sb[:, m * 64:(m + 1) * 64], in0=jf_sb[:],
                    scalar1=float(m), scalar2=None, op0=ALU.is_equal),
                    waits=[("DV", DV_R1)] if m == 0 else (), inc="DV")
            for m in range(8):
                iyb = iyf_sb[:, m::8].unsqueeze(2).to_broadcast([128, 8, 8])
                emit("vector", lambda v, m=m, iyb=iyb: v.scalar_tensor_tensor(
                    out=term_sb[:, m * 64:(m + 1) * 64].rearrange(
                        "p (h j) -> p h j", h=8),
                    in0=eqb_sb[:, m * 64:(m + 1) * 64].rearrange(
                        "p (h j) -> p h j", h=8),
                    scalar=1.0, in1=iyb, op0=ALU.mult, op1=ALU.mult),
                    waits=[("DV", cur("DV"))] if m == 0 else (),
                    inc="DV")
            DV_T2 = cur("DV")
            emit("vector", lambda v: v.reduce_sum(
                out=if_sb[:],
                in_=term_sb[:].rearrange("p (m x) -> p x m", m=8), axis=AX.X),
                waits=[("DV", DV_T2)], inc="DV")
            emit("vector", lambda v, r=r: v.tensor_add(
                out=ef_sb[r % 2][:], in0=ef_sb[r % 2][:], in1=if_sb[:]),
                waits=[("DV", cur("DV"))], inc="DV")
            emit("vector", lambda v, r=r: v.tensor_copy(
                out=ei_sb[r % 2][:], in_=ef_sb[r % 2][:]),
                waits=[("DV", cur("DV"))], inc="DV")
            DV_IDX[r] = cur("DV")
            emit("vector", lambda v: v.tensor_tensor(
                out=se_sb[:].rearrange("p (h j) -> p h j", h=8),
                in0=s8_sb[:].rearrange("p (h j) -> p h j", h=8),
                in1=s8_sb[:, 0::8].unsqueeze(2).to_broadcast([128, 8, 8]),
                op=ALU.subtract), waits=[("DV", cur("DV"))], inc="DV")
            DV_SE = cur("DV")
            emit("scalar", lambda s: s.activation(
                out=se_sb[:], in_=se_sb[:], func=AF.Exp),
                waits=[("DV", DV_SE)], inc="AC")
            AC_SE = cur("AC")
            emit("vector", lambda v: v.reduce_sum(
                out=zd_sb[:], in_=se_sb[:].rearrange("p (h j) -> p h j", h=8),
                axis=AX.X), waits=[("AC", AC_SE), ("DV", cur("DV"))], inc="DV")
            emit("vector", lambda v: v.reciprocal(out=zd_sb[:], in_=zd_sb[:]),
                 waits=[("DV", cur("DV"))], inc="DV")
            emit("vector", lambda v, r=r: v.tensor_tensor(
                out=sg_sb[r % 2][:].rearrange("p (h j) -> p h j", h=8),
                in0=se_sb[:].rearrange("p (h j) -> p h j", h=8),
                in1=zd_sb[:].unsqueeze(2).to_broadcast([128, 8, 8]),
                op=ALU.mult), waits=[("DV", cur("DV"))], inc="DV")
            DV_SIG[r] = cur("DV")

        def ffnb_stt(r):
            if not (0 <= r <= RT - 1):
                return
            for j in range(NSLOT):
                bi = j % 3
                pj = (r, j - 3) if j >= 3 else (r - 1, [63, 61, 62][j])
                emit("gpsimd", lambda g, r=r, j=j, bi=bi: g.indirect_dma_start(
                    out=gbw[bi][:], out_offset=None, in_=embd,
                    in_offset=IndirectOffsetOnAxis(ei_sb[r % 2][:, j:j + 1], 0)),
                    waits=[("DV", DV_IDX[r]), ("DV", STT_J.get(pj, 0))],
                    inc=f"GW{bi}", incn=16)
                GW_J[(r, j)] = (f"GW{bi}", cur(f"GW{bi}"))
                wstt = [GW_J[(r, j)], LD_XNR[r], ("DV", cur("DV"))]
                if j == 0 and (r - 1) in SA_CNT:
                    wstt.append(SA_CNT[r - 1])
                emit("vector", lambda v, r=r, j=j, bi=bi: v.scalar_tensor_tensor(
                    out=M_sb[:, 0:D], in0=gbw[bi][:], scalar=1.0,
                    in1=xnr_sb[r % 3][:], op0=ALU.mult, op1=ALU.mult,
                    accum_out=h_sb[:, j:j + 1]),
                    waits=wstt,
                    inc="DV")
                STT_J[(r, j)] = cur("DV")
            DV_STT[r] = cur("DV")

        def ffnb_gelu(r):
            if not (0 <= r <= RT - 1):
                return
            emit("vector", lambda v: v.tensor_tensor(
                out=g1_sb[:], in0=h_sb[:], in1=h_sb[:], op=ALU.mult),
                waits=[("DV", DV_STT[r])], inc="DV")
            emit("vector", lambda v: v.scalar_tensor_tensor(
                out=g2_sb[:], in0=g1_sb[:], scalar=0.044715, in1=h_sb[:],
                op0=ALU.mult, op1=ALU.mult), waits=[("DV", cur("DV"))],
                inc="DV")
            emit("vector", lambda v: v.tensor_add(
                out=g3_sb[:], in0=g2_sb[:], in1=h_sb[:]),
                waits=[("DV", cur("DV"))], inc="DV")
            DV_Z = cur("DV")
            emit("scalar", lambda s: s.activation(
                out=g3_sb[:], in_=g3_sb[:], func=AF.Exp, scale=2.0 * GELC),
                waits=[("DV", DV_Z)], inc="AC")
            emit("scalar", lambda s: s.activation(
                out=g3_sb[:], in_=g3_sb[:], func=AF.Identity, bias=one_t[:]),
                waits=[("AC", cur("AC"))], inc="AC")
            AC_X2 = cur("AC")
            emit("vector", lambda v: v.reciprocal(out=g1_sb[:], in_=g3_sb[:]),
                 waits=[("AC", AC_X2), ("DV", cur("DV"))], inc="DV")
            emit("vector", lambda v, r=r: v.tensor_tensor(
                out=g2_sb[:], in0=sg_sb[r % 2][:], in1=h_sb[:], op=ALU.mult),
                waits=[("DV", DV_SIG[r]), ("DV", cur("DV"))], inc="DV")
            emit("vector", lambda v: v.tensor_tensor(
                out=g1_sb[:], in0=g1_sb[:], in1=g2_sb[:], op=ALU.mult),
                waits=[("DV", cur("DV"))], inc="DV")
            emit("vector", lambda v: v.tensor_tensor(
                out=ht_sb[:], in0=g2_sb[:], in1=g1_sb[:], op=ALU.subtract),
                waits=[("DV", cur("DV"))], inc="DV")
            DV_HT[r] = cur("DV")

        def ffnb_dmm(r):
            if not (0 <= r <= RT - 1):
                return
            psf = ps_f[r % 2]
            for j in range(NSLOT):
                dpar = j % 2
                bi = j % 3
                pj = (r, j - 3) if j >= 3 else (r - 1, [63, 61, 62][j])
                emit("gpsimd", lambda g, r=r, j=j, bi=bi: g.indirect_dma_start(
                    out=gbu[bi][:], out_offset=None, in_=embu,
                    in_offset=IndirectOffsetOnAxis(ei_sb[r % 2][:, j:j + 1], 0)),
                    waits=[("DV", DV_IDX[r]), ("PE", PE_FMM_T.get(pj, 0))],
                    inc=f"GU{bi}", incn=16)
                GU_J[(r, j)] = (f"GU{bi}", cur(f"GU{bi}"))
                pjD = (r, j - 2) if j >= 2 else (r - 1, 62 + j)
                emit("vector", lambda v, r=r, j=j, dpar=dpar: v.tensor_scalar(
                    out=D_sb[dpar], in0=dg_bf[:],
                    scalar1=ht_sb[:, j:j + 1], scalar2=None, op0=ALU.mult),
                    waits=[("DV", DV_HT[r] if j < 2 else 0),
                           ("PE", PE_FMM_T.get(pjD, 0)),
                           ("DV", cur("DV") if j > 0 else 0)], inc="DV")
                dv_d = cur("DV")
                w0 = [("DV", dv_d), GU_J[(r, j)]]
                if j == 0:
                    w0.append(("DV", DV_FINB.get(r - 2, 0)))
                emit("tensor", lambda t, dpar=dpar: t.ldweights(D_sb[dpar]),
                     waits=w0)
                emit("tensor", lambda t, j=j, bi=bi, psf=psf, dpar=dpar: t.matmul(
                    out=psf[0], lhsT=D_sb[dpar],
                    rhs=gbu[bi][:, 0:512],
                    start=(j == 0), stop=(j == NSLOT - 1)))
                emit("tensor", lambda t, j=j, bi=bi, psf=psf, dpar=dpar: t.matmul(
                    out=psf[1], lhsT=D_sb[dpar],
                    rhs=gbu[bi][:, 512:1024],
                    start=(j == 0), stop=(j == NSLOT - 1)), inc="PE")
                PE_FMM_T[(r, j)] = cur("PE")
            PE_FMM[r] = cur("PE")

        K_FIN = int(os.environ.get("K_FIN", "99"))

        def fin_stage(r):
            if not (0 <= r <= RT - 1):
                return
            fincut = K_FIN if r >= 1 else 99
            psf = ps_f[r % 2]
            opar = r % 2
            emit("vector", lambda v, r=r, psf=psf, opar=opar:
                 v.scalar_tensor_tensor(
                     out=outp_sb[opar][:, 0:512], in0=psf[0], scalar=1.0,
                     in1=xr_sb[r % 2][:, 0:512], op0=ALU.mult, op1=ALU.add),
                 waits=[("PE", PE_FMM[r]), LD_XR[r], ("DV", cur("DV"))]
                 + ([ST_CNT[r - 2]] if (r - 2) in ST_CNT else []),
                 inc="DV")
            emit("vector", lambda v, r=r, psf=psf, opar=opar:
                 v.scalar_tensor_tensor(
                     out=outp_sb[opar][:, 512:1024], in0=psf[1],
                     scalar=1.0, in1=xr_sb[r % 2][:, 512:1024], op0=ALU.mult,
                     op1=ALU.add), inc="DV")
            DV_FINB[r] = cur("DV")
            if fincut < 2:
                return
            ash = M_sb[:].bitcast(F32)
            asem = f"SA{r % 2}"
            for h in range(16):
                w = [("DV", DV_AFR.get(r + 2, 0)), ("DV", DV_O.get(r, 0)),
                     ("DV", DV_O.get(r + 1, 0))] if h == 0 else ()
                emit("sync", lambda s, r=r, h=h, ash=ash: s.dma_start(
                    out=ash[0:128 - h, h * 64:(h + 1) * 64],
                    in_=O_sb[r % 3][h:128, h * 64:(h + 1) * 64]),
                    waits=w, inc=asem, incn=16)
                if h > 0:
                    emit("sync", lambda s, r=r, h=h, ash=ash: s.dma_start(
                        out=ash[128 - h:128, h * 64:(h + 1) * 64],
                        in_=O_sb[(r + 1) % 3][0:h, h * 64:(h + 1) * 64]),
                        inc=asem, incn=16)
            SA_CNT[r] = (asem, cur(asem))
            emit("vector", lambda v, r=r, opar=opar, ash=ash: v.tensor_tensor(
                out=outp_sb[opar][:, 0:512], in0=outp_sb[opar][:, 0:512],
                in1=ash[:, 0:512], op=ALU.add),
                waits=[SA_CNT[r], ("DV", cur("DV"))], inc="DV")
            emit("vector", lambda v, r=r, opar=opar, ash=ash: v.tensor_tensor(
                out=outp_sb[opar][:, 512:1024], in0=outp_sb[opar][:, 512:1024],
                in1=ash[:, 512:1024], op=ALU.add),
                waits=[("DV", cur("DV"))], inc="DV")
            DV_FINB[r] = cur("DV")
            if fincut < 3:
                return
            emit("sync", lambda s, r=r, opar=opar: s.dma_start(
                out=out_d[r * 128:(r + 1) * 128, :], in_=outp_sb[opar][:]),
                waits=[("DV", DV_FINB[r])], inc=f"ST{r % 2}", incn=16)
            ST_CNT[r] = (f"ST{r % 2}", cur(f"ST{r % 2}"))

        NIT = PT + 3
        if K_STAGE == "p0" or K_CUT < 6:
            NIT = -1
        elif K_STAGE:
            NIT = int(K_STAGE)
        if NIT >= 0:
            load_stage(0)
            qkv_stage(0)
        K_LAST = int(os.environ.get("K_LAST", "99"))
        for it in range(0, NIT):
            last = (it == NIT - 1)

            def _on(k):
                return (not last) or K_LAST >= k
            if _on(0):
                load_stage(it + 1)
            if _on(1):
                qkv_stage(it + 1)
            if _on(2):
                att_strips(it)
            if _on(3):
                ffnb_stt(it - 1)
            if _on(4):
                ffnb_gelu(it - 1)
            if _on(5):
                att_dots(it)
            if _on(6):
                ffnh_stage(it)
            if _on(7):
                ffnb_dmm(it - 1)
            if _on(8):
                fin_stage(it - 2)
            if _on(9):
                xr_load(it)

        allsems = [(s, cur(s)) for s in SEMS if cur(s) > 0 and s != "ST0"]
        if os.environ.get("K_DUMPSEM"):
            print("SEMMAX", sorted(((cur(s), s) for s in SEMS), reverse=True)[:10])
        emit("sync", lambda s: s.wait_ge(semh["ST0"], cur("ST0")),
             waits=allsems)

        with nc.Block() as block:
            @block.sync
            def _(eng):
                for f in ops["sync"]:
                    f(eng)

            @block.vector
            def _(eng):
                for f in ops["vector"]:
                    f(eng)

            @block.scalar
            def _(eng):
                for f in ops["scalar"]:
                    f(eng)

            @block.gpsimd
            def _(eng):
                for f in ops["gpsimd"]:
                    f(eng)

            @block.tensor
            def _(eng):
                for f in ops["tensor"]:
                    f(eng)

    return nc


def kernel(**inputs):
    x = np.ascontiguousarray(np.asarray(inputs["L0_attn_input_seq"],
                                        dtype=np.float32))
    Wq = np.asarray(inputs["L0_attn_Wq"], dtype=np.float32)
    bq = np.ascontiguousarray(np.asarray(inputs["L0_attn_bq"],
                                         dtype=np.float32)).reshape(1, D)
    Wk = np.asarray(inputs["L0_attn_Wk"], dtype=np.float32)
    bk = np.ascontiguousarray(np.asarray(inputs["L0_attn_bk"],
                                         dtype=np.float32)).reshape(1, D)
    Wv = np.asarray(inputs["L0_attn_Wv"], dtype=np.float32)
    bv = np.ascontiguousarray(np.asarray(inputs["L0_attn_bv"],
                                         dtype=np.float32)).reshape(1, D)
    w_q = np.ascontiguousarray(np.asarray(inputs["L0_ffn_w_q"],
                                          dtype=np.float32))
    gam = np.asarray(inputs["L0_ffn_bn_gamma"], dtype=np.float32)
    bet = np.asarray(inputs["L0_ffn_bn_beta"], dtype=np.float32)
    keys = np.asarray(inputs["L0_ffn_keys"], dtype=np.float32)
    down = np.ascontiguousarray(np.asarray(inputs["L0_ffn_down_embed"],
                                           dtype=np.float32))
    up = np.ascontiguousarray(np.asarray(inputs["L0_ffn_up_embed"],
                                         dtype=np.float32))

    xf = np.ascontiguousarray(x.reshape(NTOK, D))
    wqTh = np.ascontiguousarray(Wq.T)
    wkTh = np.ascontiguousarray(Wk.T)
    wvTh = np.ascontiguousarray(Wv.T)
    ktr = np.ascontiguousarray(keys.transpose(2, 0, 3, 1))
    gcol = np.ascontiguousarray(gam.reshape(16, 128).T)
    bcol = np.ascontiguousarray(bet.reshape(16, 128).T)

    nc = build_nc()
    in_maps = []
    for c in range(NCORE):
        b = c // 4
        g0 = b * S + (c % 4) * TPC
        seq0 = b * S
        lo = max(g0 - 15, seq0)
        hi = g0 + TPC
        g = np.arange(NTOK).reshape(GT, 128).T
        ok = (g >= lo) & (g < hi)
        smap = np.where(ok, g - g0 + 15, PADROWS).astype(np.int32)
        in_maps.append(dict(
            xf=xf, xr_in=np.ascontiguousarray(xf[g0:g0 + TPC]),
            smap=np.ascontiguousarray(smap),
            wqT=wqTh, wkT=wkTh, wvT=wvTh, wqf=w_q, ktr=ktr,
            bq_d=bq, bk_d=bk, bv_d=bv,
            gcol_d=gcol, bcol_d=bcol, embd=down, embu=up,
        ))
    res = run_bass_kernel_spmd(nc, in_maps, core_ids=list(range(NCORE)))
    out = np.concatenate([np.asarray(res.results[c]["out"])
                          for c in range(NCORE)], axis=0)
    return out.reshape(B, S, D)
